# version 04
# baseline (speedup 1.0000x reference)
import numpy as np
import ml_dtypes
import concourse.bass as bass
import concourse.mybir as mybir
from concourse.bass_utils import run_bass_kernel_spmd

F32 = mybir.dt.float32
BF16 = mybir.dt.bfloat16
I32 = mybir.dt.int32
U8 = mybir.dt.uint8
AF = mybir.ActivationFunctionType
ALU = mybir.AluOpType
AX = mybir.AxisListType.X

D = 4096
II = 11008
H = 32
HD = 128
TOK = 1024
NCORE = 8
IT = II // 128
DT = D // 128
NGD = D // 64
NGI = II // 64
SEQB = 4096
NB = SEQB // 128
EXP_BIAS = -32.0

ENGINES = ("sync", "tensor", "vector", "scalar", "gpsimd")
ACOLS = 46000

R_A = 0
R_B = 1504
R_C = 9712
R_D = 31750


class Sem:
    __slots__ = ("h", "n")

    def __init__(self, h):
        self.h = h
        self.n = 0


class Prog:
    def __init__(self, nc):
        self.nc = nc
        self.ops = {e: [] for e in ENGINES}
        self._sems = {}

    def sem(self, name):
        if name not in self._sems:
            self._sems[name] = Sem(self.nc.alloc_semaphore(name))
        return self._sems[name]

    def op(self, eng, fn):
        self.ops[eng].append(fn)

    def opi(self, eng, sem, fn, inc=1):
        sem.n += inc
        self.ops[eng].append(lambda e, f=fn, h=sem.h, i=inc: f(e).then_inc(h, i))

    def dma(self, eng, dst, src, sem):
        sem.n += 16
        self.ops[eng].append(
            lambda e, d=dst, s=src, h=sem.h: e.dma_start(out=d, in_=s).then_inc(h, 16))

    def wait(self, eng, sem, val):
        if val <= 0:
            return
        self.ops[eng].append(lambda e, h=sem.h, v=val: e.wait_ge(h, v))

    def pre(self, eng, lst):
        for sem, val in lst or ():
            self.wait(eng, sem, val)

    def finalize(self):
        with self.nc.Block() as block:
            block.sync(lambda e, ops=self.ops["sync"]: [f(e) for f in ops])
            block.tensor(lambda e, ops=self.ops["tensor"]: [f(e) for f in ops])
            block.vector(lambda e, ops=self.ops["vector"]: [f(e) for f in ops])
            block.scalar(lambda e, ops=self.ops["scalar"]: [f(e) for f in ops])
            block.gpsimd(lambda e, ops=self.ops["gpsimd"]: [f(e) for f in ops])


class Mem:
    def __init__(self, nc):
        self.t = nc.alloc_sbuf_tensor("arena", [128, ACOLS], F32)
        self.f32 = self.t.ap()
        self.bf16 = self.f32.bitcast(BF16)
        self.i32 = self.f32.bitcast(I32)
        self.u8 = self.f32.bitcast(U8)

    def v(self, off, n, dt=F32):
        assert off + n <= ACOLS, (off, n)
        if dt == F32:
            return self.f32[:, off:off + n]
        if dt == BF16:
            return self.bf16[:, 2 * off:2 * (off + n)]
        if dt == I32:
            return self.i32[:, off:off + n]
        if dt == U8:
            return self.u8[:, 4 * off:4 * (off + n)]
        raise ValueError(dt)


class Cursor:
    def __init__(self, mem, off, size):
        self.mem = mem
        self.off = off
        self.end = off + size

    def al(self, n, dt=F32):
        off = (self.off + 15) // 16 * 16
        self.off = off + n
        assert self.off <= self.end, f"region overflow {self.off} > {self.end}"
        return self.mem.v(off, n, dt)


def bcast_g(ap2d, ng, w):
    return ap2d.unsqueeze(2).broadcast_to([128, ng, w])


def gen_dequant(p, cur, ident_bf, jobs, rows, cols, piece, name, gctr,
                pre_g=None):
    njob = len(jobs)
    nrt = rows // 128
    ng = cols // 64
    npc = cols // piece
    ngp = piece // 64
    nct = cols // 128
    grp_sizes = [min(4, nct - 4 * g) for g in range((nct + 3) // 4)]
    ngrp = len(grp_sizes)

    s_pk = [p.sem(name + "pk0"), p.sem(name + "pk1")]
    s_dq = p.sem(name + "dq")
    s_tp = p.sem("Gtp")
    s_ev = p.sem("Gev")
    outs = [p.sem(j[4]) for j in jobs]

    pk = [cur.al(cols // 8, U8) for _ in range(2)]
    sz = [cur.al(2 * ng) for _ in range(2)]
    nu8 = cur.al(cols // 4, U8)
    tmp = cur.al(piece)
    wnat = cur.al(cols // 2, BF16)
    stage = cur.al(cols // 2, BF16)
    wnat_tp = {}

    p.pre("gpsimd", pre_g)
    step = 0
    for k in range(nrt):
        for ji in range(njob):
            wq_u8, wsc, wzr, wT_dram, _ = jobs[ji]
            b = step % 2
            s_out = outs[ji]
            p.wait("gpsimd", s_dq, step - 1)
            p.dma("gpsimd", pk[b], wq_u8[128 * k:128 * (k + 1), :], s_pk[b])
            p.dma("gpsimd", sz[b][:, 0:ng], wsc[128 * k:128 * (k + 1), :], s_pk[b])
            p.dma("gpsimd", sz[b][:, ng:2 * ng], wzr[128 * k:128 * (k + 1), :], s_pk[b])
            p.wait("vector", s_pk[b], s_pk[b].n)
            nv = nu8.rearrange("p (g two c) -> p g two c", two=2, c=32)
            pv = pk[b].rearrange("p (g c) -> p g c", c=32)
            p.op("vector", lambda v, o=nv, i=pv: v.tensor_single_scalar(
                out=o[:, :, 0, :], in_=i, scalar=4, op=ALU.logical_shift_right))
            p.op("vector", lambda v, o=nv, i=pv: v.tensor_single_scalar(
                out=o[:, :, 1, :], in_=i, scalar=15, op=ALU.bitwise_and))
            if step >= 1:
                p.wait("vector", s_tp, wnat_tp[step - 1] + ngrp)
            for pc in range(npc):
                g0 = pc * ngp
                nvv = nu8[:, pc * piece:(pc + 1) * piece].rearrange("p (g c) -> p g c", c=64)
                tv = tmp.rearrange("p (g c) -> p g c", c=64)
                wv = wnat[:, pc * piece:(pc + 1) * piece].rearrange("p (g c) -> p g c", c=64)
                zb = bcast_g(sz[b][:, ng + g0:ng + g0 + ngp], ngp, 64)
                sb_ = bcast_g(sz[b][:, g0:g0 + ngp], ngp, 64)
                p.op("vector", lambda v, o=tv, a=nvv, c=zb: v.tensor_tensor(
                    out=o, in0=a, in1=c, op=ALU.subtract))
                fn = lambda v, o=wv, a=tv, c=sb_: v.tensor_tensor(out=o, in0=a, in1=c, op=ALU.mult)
                if pc == npc - 1:
                    p.opi("vector", s_dq, fn)
                else:
                    p.op("vector", fn)
            wnat_tp[step] = gctr["tp"]
            p.wait("tensor", s_dq, step + 1)
            for gg in range(ngrp):
                gid = gctr["tp"]
                gctr["tp"] += 1
                bank = gctr["banks"][gid % 2]
                p.wait("tensor", s_ev, gid - 1)
                jend = 4 * gg + grp_sizes[gg]
                for j in range(4 * gg, jend):
                    fn = lambda t, o=bank[:, 128 * (j - 4 * gg):128 * (j - 4 * gg + 1)], i=wnat[:, 128 * j:128 * (j + 1)], idn=ident_bf: t.transpose(o, i, idn)
                    if j == jend - 1:
                        p.opi("tensor", s_tp, fn)
                    else:
                        p.op("tensor", fn)
            if step >= 1:
                ps_ = outs[(step - 1) % njob]
                p.wait("scalar", ps_, 16 * ((step - 1) // njob + 1))
            for gg in range(ngrp):
                gid = gctr["ev"]
                gctr["ev"] += 1
                p.wait("scalar", s_tp, gid + 1)
                bank = gctr["banks"][gid % 2]
                wbf = 128 * grp_sizes[gg]
                dst = stage[:, 512 * gg:512 * gg + wbf]
                p.opi("scalar", s_ev, lambda s, o=dst, i=bank[:, 0:wbf]: s.copy(out=o, in_=i))
            p.wait("sync", s_ev, s_ev.n)
            p.wait("sync", s_out, 16 * k)
            p.dma("sync", wT_dram[k], stage, s_out)
            step += 1
            yield k, ji


def emit_ln(p, cur, ident_bf, src_dram, t0, ntile, xT_sb, tw, name, gctr,
            pre_sync=None, pre_scal=None):
    s_x = p.sem(name + "x")
    s_n = p.sem(name + "n")
    s_sq = p.sem(name + "sq")
    s_s1 = p.sem(name + "s1")
    s_rs = p.sem(name + "rs")
    s_tp = p.sem("Gtp")
    s_ev = p.sem("Gev")

    off0 = cur.off
    xf = cur.al(D)
    xn = [cur.al(D // 2, BF16) for _ in range(2)]
    st = cur.al(16)
    m_ap, sq_ap, var_ap, rstd_ap = st[:, 0:1], st[:, 1:2], st[:, 2:3], st[:, 3:4]
    t4, t5 = st[:, 4:5], st[:, 5:6]

    p.pre("sync", pre_sync)
    p.pre("scalar", pre_scal)
    ln_tp = {}
    for i in range(ntile):
        b = i % 2
        p.wait("sync", s_n, 2 * i)
        p.dma("sync", xf, src_dram[128 * (t0 + i):128 * (t0 + i + 1), :], s_x)
        p.wait("vector", s_x, 16 * (i + 1))
        vc = p.sem(name + "vc")
        scn = p.sem(name + "scn")
        p.opi("vector", vc, lambda v: v.reduce_sum(out=m_ap, in_=xf, axis=AX))
        p.wait("vector", vc, vc.n)
        p.opi("vector", s_s1, lambda v: v.tensor_scalar_mul(out=m_ap, in0=m_ap, scalar1=1.0 / D))
        p.wait("scalar", s_x, 16 * (i + 1))
        p.wait("scalar", s_s1, 2 * i + 1)
        if i >= 1:
            p.wait("scalar", s_tp, ln_tp[i - 1] + DT // 4)
        p.opi("scalar", s_sq, lambda s, o=xn[b]: s.activation(
            out=o, in_=xf, func=AF.Square, accum_out=sq_ap))
        p.wait("vector", s_sq, i + 1)
        p.opi("vector", vc, lambda v: v.tensor_scalar_mul(out=var_ap, in0=sq_ap, scalar1=1.0 / D))
        p.opi("vector", vc, lambda v: v.tensor_mul(out=t4, in0=m_ap, in1=m_ap))
        p.wait("vector", vc, vc.n)
        p.opi("vector", vc, lambda v: v.tensor_sub(out=var_ap, in0=var_ap, in1=t4))
        p.wait("vector", vc, vc.n)
        p.opi("vector", s_s1, lambda v: v.tensor_scalar_add(out=var_ap, in0=var_ap, scalar1=1e-5))
        p.wait("scalar", s_s1, 2 * (i + 1))
        p.opi("scalar", scn, lambda s: s.activation(out=rstd_ap, in_=var_ap, func=AF.Ln))
        p.wait("scalar", scn, scn.n)
        p.opi("scalar", s_rs, lambda s: s.activation(out=rstd_ap, in_=rstd_ap, func=AF.Exp, scale=-0.5))
        p.wait("vector", s_rs, i + 1)
        p.opi("vector", vc, lambda v: v.tensor_mul(out=t5, in0=rstd_ap, in1=rstd_ap))
        p.wait("vector", vc, vc.n)
        p.opi("vector", vc, lambda v: v.tensor_mul(out=t5, in0=t5, in1=var_ap))
        p.wait("vector", vc, vc.n)
        p.opi("vector", vc, lambda v: v.tensor_scalar(out=t5, in0=t5, scalar1=-0.5, scalar2=1.5, op0=ALU.mult, op1=ALU.add))
        p.wait("vector", vc, vc.n)
        p.opi("vector", vc, lambda v: v.tensor_mul(out=rstd_ap, in0=rstd_ap, in1=t5))
        p.wait("vector", vc, vc.n)
        if i >= 2:
            p.wait("vector", s_tp, ln_tp[i - 2] + DT // 4)
        xfv = xf.rearrange("p (g c two) -> p g c two", two=2, c=32)
        xnv = xn[b].rearrange("p (g two c) -> p g two c", two=2, c=32)
        p.op("vector", lambda v, iv=xfv, ov=xnv: v.tensor_scalar(
            out=ov[:, :, 0, :], in0=iv[:, :, :, 0], scalar1=m_ap, scalar2=rstd_ap,
            op0=ALU.subtract, op1=ALU.mult))
        p.opi("vector", s_n, lambda v, iv=xfv, ov=xnv: v.tensor_scalar(
            out=ov[:, :, 1, :], in0=iv[:, :, :, 1], scalar1=m_ap, scalar2=rstd_ap,
            op0=ALU.subtract, op1=ALU.mult), inc=2)
        ln_tp[i] = gctr["tp"]
        p.wait("tensor", s_n, 2 * (i + 1))
        for gg in range(DT // 4):
            gid = gctr["tp"]
            gctr["tp"] += 1
            bank = gctr["banks"][gid % 2]
            p.wait("tensor", s_ev, gid - 1)
            for j in range(4 * gg, 4 * gg + 4):
                fn = lambda t, o=bank[:, 128 * (j - 4 * gg):128 * (j - 4 * gg + 1)], i_=xn[b][:, 128 * j:128 * (j + 1)], idn=ident_bf: t.transpose(o, i_, idn)
                if j == 4 * gg + 3:
                    p.opi("tensor", s_tp, fn)
                else:
                    p.op("tensor", fn)
        for gg in range(DT // 4):
            gid = gctr["ev"]
            gctr["ev"] += 1
            p.wait("scalar", s_tp, gid + 1)
            bank = gctr["banks"][gid % 2]
            for j in range(4 * gg, 4 * gg + 4):
                c0 = tw * j + 128 * i
                dst = xT_sb[:, c0:c0 + 128]
                fn = lambda s, o=dst, q=bank[:, 128 * (j - 4 * gg):128 * (j - 4 * gg + 1)]: s.copy(out=o, in_=q)
                if j == 4 * gg + 3:
                    p.opi("scalar", s_ev, fn)
                else:
                    p.op("scalar", fn)
    cur.off = off0
    return p.sem("Gev").n


def gen_gu(p, cur, xT_sb, wgT_dram, wuT_dram, h, psum, name,
           dep_g=None, dep_u=None, xT_dep=None, pre_sync=None, pre_v=None,
           pre_t=None):
    s_wg = p.sem(name + "wg")
    s_wu = p.sem(name + "wu")
    s_g = p.sem(name + "g")
    s_gu = p.sem(name + "gu")
    s_sl = p.sem(name + "sl")
    s_h = p.sem(name + "h")
    off0 = cur.off
    wg = cur.al(D // 2, BF16)
    wu = cur.al(D // 2, BF16)
    sl = cur.al(512)
    ps_g, ps_u = psum[2], psum[3]
    p.pre("sync", pre_sync)
    p.pre("vector", pre_v)
    p.pre("tensor", pre_t)
    for k in range(IT):
        p.wait("sync", s_g, k)
        if dep_g is not None:
            p.wait("sync", dep_g[0], dep_g[1](k))
        p.dma("sync", wg, wgT_dram[k], s_wg)
        p.wait("sync", s_gu, k)
        if dep_u is not None:
            p.wait("sync", dep_u[0], dep_u[1](k))
        p.dma("sync", wu, wuT_dram[k], s_wu)
        p.wait("tensor", s_wg, 16 * (k + 1))
        if k == 0 and xT_dep is not None:
            p.wait("tensor", xT_dep[0], xT_dep[1])
        p.wait("tensor", s_sl, k)
        for j in range(DT):
            fn = lambda t, l=wg[:, 128 * j:128 * (j + 1)], r=xT_sb[:, 512 * j:512 * (j + 1)], st_=(j == 0), sp=(j == DT - 1): t.matmul(ps_g, l, r, start=st_, stop=sp)
            if j == DT - 1:
                p.opi("tensor", s_g, fn)
            else:
                p.op("tensor", fn)
        p.wait("tensor", s_wu, 16 * (k + 1))
        p.wait("tensor", s_h, k)
        for j in range(DT):
            fn = lambda t, l=wu[:, 128 * j:128 * (j + 1)], r=xT_sb[:, 512 * j:512 * (j + 1)], st_=(j == 0), sp=(j == DT - 1): t.matmul(ps_u, l, r, start=st_, stop=sp)
            if j == DT - 1:
                p.opi("tensor", s_gu, fn)
            else:
                p.op("tensor", fn)
        p.wait("scalar", s_g, k + 1)
        p.wait("scalar", s_h, k)
        p.opi("scalar", s_sl, lambda s: s.activation(out=sl, in_=ps_g, func=AF.Silu))
        p.wait("vector", s_sl, k + 1)
        p.wait("vector", s_gu, k + 1)
        p.opi("vector", s_h, lambda v, o=h[:, 512 * k:512 * (k + 1)]: v.tensor_tensor(
            out=o, in0=ps_u, in1=sl, op=ALU.mult))
        yield k
    cur.off = off0


def emit_down(p, cur, ident_f32, h, wdT_dram, resid_dram, y_out_dram, t0,
              res_scale, psum, name, h_dep, wd_dep=None, pre_sync=None,
              pre_g=None, pre_t=None):
    s_wd = p.sem(name + "wd")
    s_dh = p.sem(name + "dh")
    s_dn = p.sem(name + "dn")
    s_de = p.sem(name + "de")
    s_dt = p.sem(name + "dt")
    s_xr = [p.sem(name + "xr0"), p.sem(name + "xr1")]
    s_y = p.sem(name + "y")
    s_yo = [p.sem(name + "yo0"), p.sem(name + "yo1")]
    s_yoc = p.sem(name + "yo")
    off0 = cur.off
    xr = [cur.al(128) for _ in range(2)]
    yst = [cur.al(128) for _ in range(2)]
    dnsb = [cur.al(512) for _ in range(2)]
    wdm = [cur.al(II // 4, BF16) for _ in range(2)]
    p.pre("sync", pre_sync)
    p.pre("gpsimd", pre_g)
    p.pre("tensor", pre_t)
    for m in range(DT):
        for hh in range(2):
            sid = 2 * m + hh
            p.wait("sync", s_dh, sid - 1)
            if wd_dep is not None:
                p.wait("sync", wd_dep[0], wd_dep[1](m))
            p.dma("sync", wdm[sid % 2],
                  wdT_dram[m, :, (II // 2) * hh:(II // 2) * (hh + 1)], s_wd)
        if m == 0:
            p.wait("tensor", h_dep[0], h_dep[1])
        p.wait("tensor", s_wd, 32 * (m + 1))
        p.wait("tensor", s_de, m - 1)
        for hh in range(2):
            sid = 2 * m + hh
            wdb = wdm[sid % 2]
            for kk in range(IT // 2):
                ki = hh * (IT // 2) + kk
                last_half = kk == IT // 2 - 1
                fn = lambda t, l=wdb[:, 128 * kk:128 * (kk + 1)], r=h[:, 512 * ki:512 * (ki + 1)], st_=(ki == 0), sp=(ki == IT - 1), pp=psum[4 + m % 2]: t.matmul(pp, l, r, start=st_, stop=sp)
                if last_half:
                    p.opi("tensor", s_dh, fn)
                else:
                    p.op("tensor", fn)
        p.wait("scalar", s_dh, 2 * (m + 1))
        p.wait("scalar", s_dt, 4 * (m - 1))
        p.opi("scalar", s_de, lambda s, o=dnsb[m % 2], i_=psum[4 + m % 2][:, 0:512]: s.copy(out=o, in_=i_))
        for tc in range(4):
            gid = m * 4 + tc
            p.wait("tensor", s_de, m + 1)
            p.wait("tensor", s_y, gid - 1)
            p.opi("tensor", s_dt, lambda t, o=psum[6 + gid % 2], i_=dnsb[m % 2][:, 128 * tc:128 * (tc + 1)], idn=ident_f32: t.transpose(o[:, 0:128], i_, idn))
            p.wait("gpsimd", s_y, gid - 1)
            p.dma("gpsimd", xr[gid % 2],
                  resid_dram[128 * (t0 + tc):128 * (t0 + tc + 1), 128 * m:128 * (m + 1)], s_xr[gid % 2])
            p.wait("vector", s_dt, gid + 1)
            p.wait("vector", s_xr[gid % 2], s_xr[gid % 2].n)
            p.wait("vector", s_yo[gid % 2], s_yo[gid % 2].n)
            p.opi("vector", s_y, lambda v, o=yst[gid % 2], a=psum[6 + gid % 2], x=xr[gid % 2], sc=float(res_scale): v.scalar_tensor_tensor(
                out=o, in0=a[:, 0:128], scalar=sc, in1=x, op0=ALU.mult, op1=ALU.add))
            p.wait("sync", s_y, gid + 1)
            p.dma("sync", y_out_dram[128 * (t0 + tc):128 * (t0 + tc + 1), 128 * m:128 * (m + 1)],
                  yst[gid % 2], s_yo[gid % 2])
    cur.off = off0
    return s_yo


def emit_qkv(p, mem, curD, x2T_sb, wq_d, wk_d, wv_d, rope_d, qT_o, kT_o, v_o,
             psum, x2T_dep, pre_t=None):
    s_w = [p.sem("qkw0"), p.sem("qkw1")]
    s_mm = p.sem("qkmm")
    s_rot = p.sem("qkrot")
    s_o = [p.sem("qko0"), p.sem("qko1")]
    s_vw = p.sem("qvw")
    s_vmm = p.sem("qvmm")
    s_ve = p.sem("qvev")
    s_vo = [p.sem("qvo0"), p.sem("qvo1")]
    s_rp = p.sem("qrope")

    curB = Cursor(mem, R_B, 8192)
    wt = [curB.al(D // 2, BF16) for _ in range(2)]
    shuf = curB.al(512)
    tmp1 = curB.al(512)
    qstage = [curB.al(256, BF16) for _ in range(2)]
    rope_sb = curB.al(2048)

    p.wait("gpsimd", x2T_dep[0], x2T_dep[1])
    p.pre("tensor", pre_t)
    for wi, (w_d, o_d) in enumerate(((wq_d, qT_o), (wk_d, kT_o))):
        if wi == 0:
            p.dma("gpsimd", rope_sb, rope_d[:, 0:2048], s_rp)
        else:
            p.wait("gpsimd", s_rot, 2 * H * wi)
            p.dma("gpsimd", rope_sb, rope_d[:, 2048:4096], s_rp)
        cosap = rope_sb[:, 0:1024]
        sinap = rope_sb[:, 1024:2048]
        for m in range(H):
            b = m % 2
            mi = wi * H + m
            p.wait("gpsimd", s_mm, 2 * (mi - 1))
            src = w_d[:, 128 * m:128 * (m + 1)].rearrange("(j p) hh -> p j hh", p=128)
            p.dma("gpsimd", wt[b].rearrange("p (j hh) -> p j hh", hh=128), src, s_w[b])
            p.wait("tensor", s_w[b], s_w[b].n)
            if mi == 0:
                p.wait("tensor", x2T_dep[0], x2T_dep[1])
            for tc in range(2):
                gid = mi * 2 + tc
                p.wait("tensor", s_rot, gid - 1)
                for j in range(DT):
                    fn = lambda t, l=wt[b][:, 128 * j:128 * (j + 1)], r=x2T_sb[:, 1024 * j + 512 * tc:1024 * j + 512 * (tc + 1)], st_=(j == 0), sp=(j == DT - 1), pp=psum[2 + tc]: t.matmul(pp, l, r, start=st_, stop=sp)
                    if j == DT - 1:
                        p.opi("tensor", s_mm, fn)
                    else:
                        p.op("tensor", fn)
                p.wait("vector", s_mm, gid + 1)
                p.wait("vector", s_o[tc], s_o[tc].n)
                if (mi == 0 and tc == 0) or (wi == 1 and m == 0 and tc == 0):
                    p.wait("vector", s_rp, 16 * (wi + 1))
                pp = psum[2 + tc]
                cs = cosap[:, 512 * tc:512 * (tc + 1)]
                sn = sinap[:, 512 * tc:512 * (tc + 1)]
                p.op("vector", lambda v, s=pp: v.tensor_copy(out=shuf[0:64, :], in_=s[64:128, 0:512]))
                p.op("vector", lambda v, s=pp: v.tensor_copy(out=shuf[64:128, :], in_=s[0:64, 0:512]))
                p.op("vector", lambda v, s=pp, c=cs: v.tensor_tensor(out=tmp1, in0=s[:, 0:512], in1=c, op=ALU.mult))
                p.op("vector", lambda v, sn_=sn: v.tensor_tensor(out=shuf, in0=shuf, in1=sn_, op=ALU.mult))
                p.opi("vector", s_rot, lambda v, q=qstage[tc]: v.tensor_tensor(out=q, in0=tmp1, in1=shuf, op=ALU.add))
                p.wait("sync", s_rot, gid + 1)
                p.dma("sync", o_d[128 * m:128 * (m + 1), 512 * tc:512 * (tc + 1)], qstage[tc], s_o[tc])
    off0 = curD.off
    wv_sb = curD.al(8192, BF16)
    vstage = [curD.al(256, BF16) for _ in range(2)]
    p.wait("gpsimd", x2T_dep[0], x2T_dep[1])
    for n in range(8):
        p.wait("gpsimd", s_vmm, 8 * n)
        src = wv_d[:, 512 * n:512 * (n + 1)].rearrange("(j p) hh -> p j hh", p=128)
        p.dma("gpsimd", wv_sb.rearrange("p (j hh) -> p j hh", hh=512), src, s_vw)
        p.wait("tensor", s_vw, 16 * (n + 1))
        for tt in range(8):
            gid = n * 8 + tt
            p.wait("tensor", s_ve, gid - 1)
            for j in range(DT):
                fn = lambda t, l=x2T_sb[:, 1024 * j + 128 * tt:1024 * j + 128 * (tt + 1)], r=wv_sb[:, 512 * j:512 * (j + 1)], st_=(j == 0), sp=(j == DT - 1), pp=psum[4 + gid % 2]: t.matmul(pp, l, r, start=st_, stop=sp)
                if j == DT - 1:
                    p.opi("tensor", s_vmm, fn)
                else:
                    p.op("tensor", fn)
            p.wait("scalar", s_vmm, gid + 1)
            p.wait("scalar", s_vo[gid % 2], s_vo[gid % 2].n)
            p.opi("scalar", s_ve, lambda s, o=vstage[gid % 2], i_=psum[4 + gid % 2]: s.copy(out=o, in_=i_[:, 0:512]))
            p.wait("sync", s_ve, gid + 1)
            p.dma("sync", v_o[128 * tt:128 * (tt + 1), 512 * n:512 * (n + 1)], vstage[gid % 2], s_vo[gid % 2])
    curD.off = off0


def build_l1():
    nc = bass.Bass("TRN2", target_bir_lowering=False, debug=False, detect_race_conditions=False)
    x_d = nc.dram_tensor("x", [TOK, D], F32, kind="ExternalInput").ap()
    wgq = nc.dram_tensor("wgq", [II, D // 2], U8, kind="ExternalInput").ap()
    wuq = nc.dram_tensor("wuq", [II, D // 2], U8, kind="ExternalInput").ap()
    wdq = nc.dram_tensor("wdq", [D, II // 2], U8, kind="ExternalInput").ap()
    wgs = nc.dram_tensor("wgs", [II, NGD], F32, kind="ExternalInput").ap()
    wgz = nc.dram_tensor("wgz", [II, NGD], F32, kind="ExternalInput").ap()
    wus = nc.dram_tensor("wus", [II, NGD], F32, kind="ExternalInput").ap()
    wuz = nc.dram_tensor("wuz", [II, NGD], F32, kind="ExternalInput").ap()
    wds = nc.dram_tensor("wds", [D, NGI], F32, kind="ExternalInput").ap()
    wdz = nc.dram_tensor("wdz", [D, NGI], F32, kind="ExternalInput").ap()
    wq_d = nc.dram_tensor("wq", [D, D], F32, kind="ExternalInput").ap()
    wk_d = nc.dram_tensor("wk", [D, D], F32, kind="ExternalInput").ap()
    wv_d = nc.dram_tensor("wv", [D, D], F32, kind="ExternalInput").ap()
    rope_d = nc.dram_tensor("rope", [128, 4096], F32, kind="ExternalInput").ap()
    iden_d = nc.dram_tensor("iden", [128, 128], F32, kind="ExternalInput").ap()

    y1_o = nc.dram_tensor("y1", [TOK, D], F32, kind="ExternalOutput").ap()
    qT_o = nc.dram_tensor("qT", [D, TOK], BF16, kind="ExternalOutput").ap()
    kT_o = nc.dram_tensor("kT", [D, TOK], BF16, kind="ExternalOutput").ap()
    v_o = nc.dram_tensor("v", [TOK, D], BF16, kind="ExternalOutput").ap()
    wgT_o = nc.dram_tensor("wgT", [IT, 128, D], BF16, kind="ExternalOutput").ap()
    wuT_o = nc.dram_tensor("wuT", [IT, 128, D], BF16, kind="ExternalOutput").ap()
    wdT_o = nc.dram_tensor("wdT", [DT, 128, II], BF16, kind="ExternalOutput").ap()

    p = Prog(nc)
    mem = Mem(nc)
    psum_bf = [nc.alloc_psum_tensor(f"pb{i}", [128, 1024], BF16).ap() for i in range(2)]
    psum = [None, None] + [nc.alloc_psum_tensor(f"ps{i}", [128, 512], F32).ap() for i in range(6)]
    gctr = {"tp": 0, "ev": 0, "banks": psum_bf}

    curA = Cursor(mem, R_A, 1500)
    ident_bf = curA.al(64, BF16)
    ident_f32 = curA.al(128)
    xT = mem.v(R_B, 8192, BF16)
    h = mem.v(R_C, 22016, BF16)
    x2T = mem.v(R_C, 16384, BF16)

    s_c = p.sem("const")
    p.dma("gpsimd", ident_bf, iden_d, s_c)
    p.dma("gpsimd", ident_f32, iden_d, s_c)
    p.wait("tensor", s_c, 32)

    curD = Cursor(mem, R_D, ACOLS - R_D)
    lnA_ev = emit_ln(p, curD, ident_bf, x_d, 0, 4, xT, 512, "lnA", gctr)
    curC = Cursor(mem, R_C, 22050)
    for _ in gen_dequant(p, curC, ident_bf,
                         [(wdq, wds, wdz, wdT_o, "WDo")], D, II, 256, "dqd", gctr):
        pass
    dq = gen_dequant(p, curD, ident_bf,
                     [(wgq, wgs, wgz, wgT_o, "WGo"), (wuq, wus, wuz, wuT_o, "WUo")],
                     II, D, 1024, "dqgu", gctr)
    guA = gen_gu(p, curD, xT, wgT_o, wuT_o, h, psum, "guA",
                 dep_g=(p.sem("WGo"), lambda k: 16 * (k + 1)),
                 dep_u=(p.sem("WUo"), lambda k: 16 * (k + 1)),
                 xT_dep=(p.sem("Gev"), lnA_ev),
                 pre_v=[(p.sem("WDo"), 16 * DT)])
    for k in range(IT):
        next(dq)
        next(dq)
        if k >= 1:
            next(guA)
    for _ in dq:
        pass
    for _ in guA:
        pass
    curD.off = R_D
    emit_down(p, curD, ident_f32, h, wdT_o, x_d, y1_o, 0, 0.5, psum, "dnA",
              h_dep=(p.sem("guAh"), IT),
              wd_dep=(p.sem("WDo"), lambda m: 16 * DT),
              pre_sync=[(p.sem("guAgu"), IT)],
              pre_g=[(p.sem("dqgudq"), 2 * IT)])
    lnB_ev = emit_ln(p, curD, ident_bf, x_d, 4, 4, xT, 512, "lnB", gctr,
                     pre_sync=[(p.sem("dnAdh"), 2 * DT), (p.sem("dnAy"), 4 * DT)],
                     pre_scal=[(p.sem("guAgu"), IT)])
    p.wait("vector", p.sem("dnAdh"), 2 * DT)
    guB = gen_gu(p, curD, xT, wgT_o, wuT_o, h, psum, "guB",
                 xT_dep=(p.sem("Gev"), lnB_ev),
                 pre_sync=[(p.sem("Gev"), lnB_ev)],
                 pre_t=[(p.sem("guAh"), IT)])
    for _ in guB:
        pass
    emit_down(p, curD, ident_f32, h, wdT_o, x_d, y1_o, 4, 0.5, psum, "dnB",
              h_dep=(p.sem("guBh"), IT),
              pre_sync=[(p.sem("guBgu"), IT)],
              pre_g=[(p.sem("guBgu"), IT)],
              pre_t=[(p.sem("dnAy"), 4 * DT)])
    ln2_ev = emit_ln(p, curD, ident_bf, y1_o, 0, 8, x2T, 1024, "ln2", gctr,
                     pre_sync=[(p.sem("dnAyo0"), 16 * 2 * DT),
                               (p.sem("dnAyo1"), 16 * 2 * DT),
                               (p.sem("dnByo0"), 16 * 2 * DT),
                               (p.sem("dnByo1"), 16 * 2 * DT),
                               (p.sem("dnBdh"), 2 * DT), (p.sem("dnBy"), 4 * DT)],
                     pre_scal=[(p.sem("dnBdh"), 2 * DT)])
    emit_qkv(p, mem, curD, x2T, wq_d, wk_d, wv_d, rope_d, qT_o, kT_o, v_o,
             psum, (p.sem("Gev"), ln2_ev),
             pre_t=[(p.sem("guBh"), IT), (p.sem("dnBy"), 4 * DT),
                    (p.sem("dnBde"), DT)])
    for nm in ("qko0", "qko1", "qvo0", "qvo1", "dnAyo0", "dnAyo1", "dnByo0",
               "dnByo1", "WGo", "WUo", "WDo"):
        p.wait("sync", p.sem(nm), p.sem(nm).n)
    p.finalize()
    return nc


def build_l2():
    nc = bass.Bass("TRN2", target_bir_lowering=False, debug=False, detect_race_conditions=False)
    y1_d = nc.dram_tensor("y1", [TOK, D], F32, kind="ExternalInput").ap()
    qT_d = nc.dram_tensor("qTt", [H, 128, TOK], BF16, kind="ExternalInput").ap()
    kT_d = nc.dram_tensor("kTt", [H, 128, SEQB], BF16, kind="ExternalInput").ap()
    v_d = nc.dram_tensor("vt", [H, NB, 128, 128], BF16, kind="ExternalInput").ap()
    wo_d = nc.dram_tensor("wo", [D, D], F32, kind="ExternalInput").ap()
    msk_d = nc.dram_tensor("msk", [4, 1024, 256], F32, kind="ExternalInput").ap()
    wgT_d = nc.dram_tensor("wgT", [IT, 128, D], BF16, kind="ExternalInput").ap()
    wuT_d = nc.dram_tensor("wuT", [IT, 128, D], BF16, kind="ExternalInput").ap()
    wdT_d = nc.dram_tensor("wdT", [DT, 128, II], BF16, kind="ExternalInput").ap()
    iden_d = nc.dram_tensor("iden", [128, 128], F32, kind="ExternalInput").ap()
    y3_o = nc.dram_tensor("y3", [TOK, D], F32, kind="ExternalOutput").ap()
    y2_s = nc.dram_tensor("y2s", [TOK, D], F32, kind="Internal").ap()

    p = Prog(nc)
    mem = Mem(nc)
    psum_bf = [nc.alloc_psum_tensor(f"pb{i}", [128, 1024], BF16).ap() for i in range(2)]
    psum = [None, None] + [nc.alloc_psum_tensor(f"ps{i}", [128, 512], F32).ap() for i in range(6)]
    gctr = {"tp": 0, "ev": 0, "banks": psum_bf}

    curA = Cursor(mem, R_A, 1500)
    ident_bf = curA.al(64, BF16)
    ident_f32 = curA.al(128)
    ones_f = curA.al(128)
    onecol_bf = curA.al(8, BF16)
    ebias = curA.al(8)
    s_c = p.sem("const")
    s_c2 = p.sem("c2")
    p.dma("gpsimd", ident_bf, iden_d, s_c)
    p.dma("gpsimd", ident_f32, iden_d, s_c)
    p.op("vector", lambda v: v.memset(ones_f, 1.0))
    p.op("vector", lambda v: v.memset(onecol_bf, 1.0))
    p.opi("vector", s_c2, lambda v: v.memset(ebias[:, 0:1], EXP_BIAS))
    p.wait("tensor", s_c, 32)
    p.wait("tensor", s_c2, 1)
    p.wait("scalar", s_c2, 1)

    pvT = mem.v(R_C, 16384, BF16)
    h = mem.v(R_C, 22016, BF16)
    xT = mem.v(R_B, 8192, BF16)

    curB = Cursor(mem, R_B, 8192)
    kv_k = [curB.al(2048, BF16) for _ in range(2)]
    kv_v = [curB.al(2048, BF16) for _ in range(2)]
    curD = Cursor(mem, R_D, ACOLS - R_D)
    q_sb = [curD.al(512, BF16) for _ in range(2)]
    mk_sb = curD.al(4096, BF16)
    probs = [curD.al(256, BF16) for _ in range(16)]
    den_f = curD.al(256)
    rb_sb = curD.al(128, BF16)

    s_kv = [p.sem("kv0"), p.sem("kv1")]
    s_q = [p.sem("q0"), p.sem("q1")]
    s_m = p.sem("mask")
    s_sc = p.sem("sc")
    s_mk = p.sem("mk")
    s_ex = p.sem("ex")
    s_pd = p.sem("pd")
    s_rc = p.sem("rc")
    s_rb = p.sem("rb")
    s_re = p.sem("re")
    s_fin = p.sem("fin")

    for o in range(4):
        p.dma("gpsimd",
              mk_sb[:, 2048 * o:2048 * (o + 1)].rearrange("p (n c) -> p n c", c=256),
              msk_d[o].rearrange("(n p) c -> p n c", p=128), s_m)

    segctr = 0
    blkctr = 0
    hoctr = 0
    seg_last_blk = {}
    blk_end = {}
    for h_i in range(H):
        p.wait("gpsimd", s_pd, blk_end.get(4 * (h_i - 2) + 3, 0))
        p.dma("gpsimd", q_sb[h_i % 2], qT_d[h_i], s_q[h_i % 2])
        for o in range(4):
            nblk = 8 * o + 8
            nseg = nblk // 2
            bb = hoctr % 2
            p.wait("sync", s_pd, blk_end.get(hoctr - 2, 0))
            p.dma("sync", kv_k[bb][:, 0:128 * nblk], kT_d[h_i, :, 0:128 * nblk], s_kv[bb])
            p.dma("sync", kv_v[bb][:, 0:128 * nblk].rearrange("p (n c) -> p n c", c=128),
                  v_d[h_i, 0:nblk].rearrange("n p c -> p n c"), s_kv[bb])
            p.wait("tensor", s_kv[bb], s_kv[bb].n)
            p.wait("tensor", s_q[h_i % 2], s_q[h_i % 2].n)
            if hoctr == 0:
                p.wait("tensor", s_m, s_m.n)
                p.wait("vector", s_m, s_m.n)
            seg0 = segctr
            for i in range(nblk):
                seg = seg0 + i // 2
                half = i % 2
                bank = psum[2 + seg % 2]
                if half == 0:
                    p.wait("tensor", s_ex, seg - 1)
                fn = lambda t, l=kv_k[bb][:, 128 * i:128 * (i + 1)], r=q_sb[h_i % 2][:, 256 * o:256 * (o + 1)], pp=bank, hf=half: t.matmul(
                    pp[:, 256 * hf:256 * (hf + 1)], l, r, start=True, stop=True)
                if half == 1:
                    p.opi("tensor", s_sc, fn)
                else:
                    p.op("tensor", fn)
            for sg in range(nseg):
                sid = seg0 + sg
                bank = psum[2 + sid % 2]
                masked = sg >= nseg - 4
                if masked:
                    mseg = sg - (nseg - 4)
                    p.wait("vector", s_sc, sid + 1)
                    p.opi("vector", s_mk, lambda v, pp=bank, mk=mk_sb[:, 2048 * o + 512 * mseg:2048 * o + 512 * (mseg + 1)]: v.tensor_tensor(out=pp, in0=pp, in1=mk, op=ALU.add))
                    p.wait("scalar", s_mk, s_mk.n)
                else:
                    p.wait("scalar", s_sc, sid + 1)
                if sid >= 16:
                    p.wait("scalar", s_pd, seg_last_blk[sid - 16])
                p.opi("scalar", s_ex, lambda s, pp=bank, o_=probs[sid % 16]: s.activation(
                    out=o_, in_=pp, func=AF.Exp, bias=ebias[:, 0:1]))
                seg_last_blk[sid] = blkctr + 2 * sg + 2
            for i in range(nblk):
                sid = seg0 + i // 2
                half = i % 2
                p.wait("tensor", s_ex, sid + 1)
                if i == 0:
                    p.wait("tensor", s_fin, hoctr)
                    p.wait("tensor", s_rc, hoctr)
                pr = probs[sid % 16].rearrange("p (hf c) -> p hf c", c=256)[:, half, :]
                p.op("tensor", lambda t, l=kv_v[bb][:, 128 * i:128 * (i + 1)], r=pr, st_=(i == 0), sp=(i == nblk - 1), pp=psum[4]: t.matmul(pp[:, 0:256], l, r, start=st_, stop=sp))
                p.opi("tensor", s_pd, lambda t, l=onecol_bf[:, 0:1], r=pr, st_=(i == 0), sp=(i == nblk - 1), pp=psum[5]: t.matmul(pp[0:1, 0:256], l, r, start=st_, stop=sp))
            segctr += nseg
            blkctr += nblk
            blk_end[hoctr] = blkctr
            p.wait("vector", s_pd, blkctr)
            p.wait("vector", s_rb, hoctr)
            p.opi("vector", s_rc, lambda v: v.reciprocal(out=den_f[0:1, :], in_=psum[5][0:1, 0:256]))
            p.wait("tensor", s_rc, hoctr + 1)
            p.wait("tensor", s_re, hoctr)
            p.opi("tensor", s_rb, lambda t: t.matmul(psum[6][:, 0:256], ones_f[0:1, 0:128], den_f[0:1, :], start=True, stop=True))
            p.wait("scalar", s_rb, hoctr + 1)
            p.wait("scalar", s_fin, hoctr)
            p.opi("scalar", s_re, lambda s: s.copy(out=rb_sb, in_=psum[6][:, 0:256]))
            p.wait("vector", s_re, hoctr + 1)
            dst = pvT[:, 1024 * h_i + 256 * o:1024 * h_i + 256 * (o + 1)]
            p.opi("vector", s_fin, lambda v, o_=dst: v.tensor_tensor(out=o_, in0=psum[4][:, 0:256], in1=rb_sb, op=ALU.mult))
            hoctr += 1
    s_ww = [p.sem("oww0"), p.sem("oww1")]
    s_omm = p.sem("omm")
    s_oe = p.sem("oev")
    s_ot = p.sem("otp")
    s_yr = [p.sem("oyr0"), p.sem("oyr1")]
    s_oy = p.sem("oy")
    s_oyo = [p.sem("oyo0"), p.sem("oyo1")]
    curO = Cursor(mem, R_D, ACOLS - R_D)
    wo_sb = [curO.al(2048, BF16) for _ in range(2)]
    osb = [curO.al(512) for _ in range(2)]
    xro = [curO.al(128) for _ in range(2)]
    ysto = [curO.al(128) for _ in range(2)]
    p.wait("gpsimd", s_fin, 4 * H)
    for dt_ in range(DT):
        b = dt_ % 2
        p.wait("gpsimd", s_omm, 2 * (dt_ - 1))
        src = wo_d[:, 128 * dt_:128 * (dt_ + 1)].rearrange("(j p) c -> p j c", p=128)
        p.dma("gpsimd", wo_sb[b].rearrange("p (j c) -> p j c", c=128), src, s_ww[b])
        p.wait("tensor", s_ww[b], s_ww[b].n)
        if dt_ == 0:
            p.wait("tensor", s_fin, 4 * H)
            p.wait("tensor", p.sem("re"), 4 * H)
            p.wait("tensor", p.sem("rc"), 4 * H)
        for tc in range(2):
            gid = 2 * dt_ + tc
            p.wait("tensor", s_oe, gid - 1)
            for hh in range(H):
                fn = lambda t, l=wo_sb[b][:, 128 * hh:128 * (hh + 1)], r=pvT[:, 1024 * hh + 512 * tc:1024 * hh + 512 * (tc + 1)], st_=(hh == 0), sp=(hh == H - 1), pp=psum[4 + tc]: t.matmul(pp, l, r, start=st_, stop=sp)
                if hh == H - 1:
                    p.opi("tensor", s_omm, fn)
                else:
                    p.op("tensor", fn)
            p.wait("scalar", s_omm, gid + 1)
            p.wait("scalar", s_oy, 4 * (gid - 1))
            p.opi("scalar", s_oe, lambda s, o=osb[gid % 2], i_=psum[4 + tc][:, 0:512]: s.copy(out=o, in_=i_))
            for q4 in range(4):
                g2 = gid * 4 + q4
                tt = tc * 4 + q4
                p.wait("tensor", s_oe, gid + 1)
                p.wait("tensor", s_oy, g2 - 1)
                p.opi("tensor", s_ot, lambda t, o=psum[6 + g2 % 2], i_=osb[gid % 2][:, 128 * q4:128 * (q4 + 1)], idn=ident_f32: t.transpose(o[:, 0:128], i_, idn))
                p.wait("gpsimd", s_oy, g2 - 1)
                p.dma("gpsimd", xro[g2 % 2], y1_d[128 * tt:128 * (tt + 1), 128 * dt_:128 * (dt_ + 1)], s_yr[g2 % 2])
                p.wait("vector", s_ot, g2 + 1)
                p.wait("vector", s_yr[g2 % 2], s_yr[g2 % 2].n)
                p.wait("vector", s_oyo[g2 % 2], s_oyo[g2 % 2].n)
                p.opi("vector", s_oy, lambda v, o=ysto[g2 % 2], a=psum[6 + g2 % 2], x=xro[g2 % 2]: v.scalar_tensor_tensor(
                    out=o, in0=a[:, 0:128], scalar=1.0, in1=x, op0=ALU.mult, op1=ALU.add))
                p.wait("sync", s_oy, g2 + 1)
                p.dma("sync", y2_s[128 * tt:128 * (tt + 1), 128 * dt_:128 * (dt_ + 1)], ysto[g2 % 2], s_oyo[g2 % 2])
    curD.off = R_D
    lnC_ev = emit_ln(p, curD, ident_bf, y2_s, 0, 4, xT, 512, "lnC", gctr,
                     pre_sync=[(p.sem("oyo0"), 16 * 128), (p.sem("oyo1"), 16 * 128),
                               (p.sem("oy"), 256), (p.sem("omm"), 2 * DT)],
                     pre_scal=[(p.sem("fin"), 4 * H), (p.sem("omm"), 2 * DT)])
    p.wait("vector", p.sem("omm"), 2 * DT)
    guC = gen_gu(p, curD, xT, wgT_d, wuT_d, h, psum, "guC",
                 xT_dep=(p.sem("Gev"), lnC_ev),
                 pre_sync=[(p.sem("Gev"), lnC_ev)],
                 pre_t=[(p.sem("fin"), 4 * H), (p.sem("rc"), 4 * H)])
    for _ in guC:
        pass
    emit_down(p, curD, ident_f32, h, wdT_d, y2_s, y3_o, 0, 0.5, psum, "dnC",
              h_dep=(p.sem("guCh"), IT),
              pre_sync=[(p.sem("guCgu"), IT)],
              pre_g=[(p.sem("guCgu"), IT)],
              pre_t=[(p.sem("oy"), 256), (p.sem("oev"), 2 * DT)])
    lnD_ev = emit_ln(p, curD, ident_bf, y2_s, 4, 4, xT, 512, "lnD", gctr,
                     pre_sync=[(p.sem("dnCdh"), 2 * DT), (p.sem("dnCy"), 4 * DT)],
                     pre_scal=[(p.sem("guCgu"), IT)])
    p.wait("vector", p.sem("dnCdh"), 2 * DT)
    guD = gen_gu(p, curD, xT, wgT_d, wuT_d, h, psum, "guD",
                 xT_dep=(p.sem("Gev"), lnD_ev),
                 pre_sync=[(p.sem("Gev"), lnD_ev)],
                 pre_t=[(p.sem("guCh"), IT)])
    for _ in guD:
        pass
    emit_down(p, curD, ident_f32, h, wdT_d, y2_s, y3_o, 4, 0.5, psum, "dnD",
              h_dep=(p.sem("guDh"), IT),
              pre_sync=[(p.sem("guDgu"), IT)],
              pre_g=[(p.sem("guDgu"), IT)],
              pre_t=[(p.sem("dnCy"), 4 * DT)])
    for nm2 in ("dnCyo0", "dnCyo1", "dnDyo0", "dnDyo1"):
        p.wait("sync", p.sem(nm2), p.sem(nm2).n)
    p.finalize()
    return nc


def _perm64(n):
    out = np.empty(n, np.int64)
    for G in range(n // 64):
        out[64 * G:64 * G + 32] = 64 * G + 2 * np.arange(32)
        out[64 * G + 32:64 * G + 64] = 64 * G + 2 * np.arange(32) + 1
    return out


def _pack_u8(w_int32):
    return np.ascontiguousarray(np.ascontiguousarray(w_int32).view(np.uint8)[:, 0::4])


_CACHE = {}


def kernel(**inp):
    pd = _perm64(D)
    pi = _perm64(II)

    wgq = np.ascontiguousarray(_pack_u8(inp["L0_ffn_gate_Wq"])[pi])
    wuq = np.ascontiguousarray(_pack_u8(inp["L0_ffn_up_Wq"])[pi])
    wdq = _pack_u8(inp["L0_ffn_down_Wq"])
    wgs = np.ascontiguousarray(inp["L0_ffn_gate_scale"][pi], np.float32)
    wgz = np.ascontiguousarray(inp["L0_ffn_gate_zero"][pi], np.float32)
    wus = np.ascontiguousarray(inp["L0_ffn_up_scale"][pi], np.float32)
    wuz = np.ascontiguousarray(inp["L0_ffn_up_zero"][pi], np.float32)
    wds = np.ascontiguousarray(inp["L0_ffn_down_scale"], np.float32)
    wdz = np.ascontiguousarray(inp["L0_ffn_down_zero"], np.float32)
    wq_p = np.ascontiguousarray(inp["L0_attn_wq"][pd])
    wk_p = np.ascontiguousarray(inp["L0_attn_wk"][pd])
    wv_p = np.ascontiguousarray(inp["L0_attn_wv"][pd])
    wo = np.ascontiguousarray(inp["L0_attn_wo"], np.float32)
    x = np.ascontiguousarray(inp["L0_attn_hidden_states"].reshape(2 * SEQB, D))
    iden = np.eye(128, dtype=np.float32)

    inv = 1.0 / 10000.0 ** (np.arange(0, HD, 2, dtype=np.float64) / HD)
    pos = np.arange(SEQB, dtype=np.float64)
    fr = pos[:, None] * inv[None, :]
    cos = np.cos(np.concatenate([fr, fr], 1)).astype(np.float32)
    sin = np.sin(np.concatenate([fr, fr], 1)).astype(np.float32)
    sin_sgn = sin.copy()
    sin_sgn[:, 0:64] *= -1.0
    sc = np.float32(1.0 / np.sqrt(HD))
    cosT = np.ascontiguousarray(cos.T)
    sinT = np.ascontiguousarray(sin_sgn.T)

    NEG = np.float32(-1e30)
    masks = []
    for r in range(4):
        mk = np.zeros((4, 1024, 256), np.float32)
        for o in range(4):
            b0 = 8 * o + 2 * r
            tq = b0 * 128 + np.arange(256)
            for tb in range(8):
                tk = (8 * o + tb) * 128 + np.arange(128)
                mk[o, tb * 128:(tb + 1) * 128, :] = np.where(
                    tk[:, None] > tq[None, :], NEG, 0.0)
        masks.append(mk)

    if "l1" not in _CACHE:
        _CACHE["l1"] = build_l1()
    maps1 = []
    for c in range(NCORE):
        seq0 = TOK * (c % 4)
        rope_c = np.concatenate([
            cosT[:, seq0:seq0 + TOK] * sc, sinT[:, seq0:seq0 + TOK] * sc,
            cosT[:, seq0:seq0 + TOK], sinT[:, seq0:seq0 + TOK]], axis=1)
        maps1.append(dict(
            x=x[TOK * c:TOK * (c + 1)], wgq=wgq, wuq=wuq, wdq=wdq, wgs=wgs,
            wgz=wgz, wus=wus, wuz=wuz, wds=wds, wdz=wdz, wq=wq_p, wk=wk_p,
            wv=wv_p, rope=np.ascontiguousarray(rope_c), iden=iden))
    r1 = run_bass_kernel_spmd(_CACHE["l1"], maps1, core_ids=list(range(NCORE)))
    y1 = np.concatenate([r1.results[c]["y1"] for c in range(NCORE)], 0)
    wgT, wuT, wdT = (r1.results[0][k] for k in ("wgT", "wuT", "wdT"))

    kT_b, v_b, qT_b = [], [], []
    for bi in range(2):
        kfull = np.concatenate([r1.results[4 * bi + r]["kT"] for r in range(4)], axis=1)
        kT_b.append(np.ascontiguousarray(kfull.reshape(H, 128, SEQB)))
        vfull = np.concatenate([r1.results[4 * bi + r]["v"] for r in range(4)], axis=0)
        v_b.append(np.ascontiguousarray(
            vfull.reshape(NB, 128, H, 128).transpose(2, 0, 1, 3)))
        qfull = np.concatenate([r1.results[4 * bi + r]["qT"] for r in range(4)], axis=1)
        qT_b.append(qfull.reshape(H, 128, SEQB))

    stripe = {}
    for r in range(4):
        blocks = []
        for o in range(4):
            blocks += [8 * o + 2 * r, 8 * o + 2 * r + 1]
        stripe[r] = np.concatenate([np.arange(128 * b, 128 * (b + 1)) for b in blocks])

    if "l2" not in _CACHE:
        _CACHE["l2"] = build_l2()
    maps2 = []
    for c in range(NCORE):
        bi, r = divmod(c, 4)
        idx = stripe[r]
        maps2.append(dict(
            y1=np.ascontiguousarray(y1[bi * SEQB + idx]),
            qTt=np.ascontiguousarray(qT_b[bi][:, :, idx]),
            kTt=kT_b[bi], vt=v_b[bi], wo=wo, msk=masks[r],
            wgT=wgT, wuT=wuT, wdT=wdT, iden=iden))
    r2 = run_bass_kernel_spmd(_CACHE["l2"], maps2, core_ids=list(range(NCORE)))

    out = np.empty((2 * SEQB, D), np.float32)
    for c in range(NCORE):
        bi, r = divmod(c, 4)
        out[bi * SEQB + stripe[r]] = r2.results[c]["y3"]
    return out.reshape(2, SEQB, D)
